# version 02
# baseline (speedup 1.0000x reference)
import numpy as np
import ml_dtypes
import concourse.bass as bass
import concourse.mybir as mybir
from concourse.bass_utils import run_bass_kernel_spmd
from contextlib import ExitStack

F32 = mybir.dt.float32
BF16 = mybir.dt.bfloat16
AF = mybir.ActivationFunctionType
ALU = mybir.AluOpType
AX = mybir.AxisListType

D = 2048
DC = 16
T = 2048
TH = 1024
H = 16
FF = 1024
FC = 8
E = 8
NEG = -1e30
SCL = float(1.0 / np.sqrt(128.0))
BIG = 1.0e6
EPS = 1e-5

ENGS = ("sync", "gpsimd", "tensor", "vector", "scalar")
DBG = False
STOP_AFTER = None
H_LIM = None
PASSB = True
I_LIM = None
PA_MASK = True
PA_EXP = True
PA_RED = True
PA_MM = True


class _StopBuild(Exception):
    pass


class Prog:
    def __init__(self):
        self.ops = {e: [] for e in ENGS}
        self.n = {}
        self.sem_names = []

    def sem(self, name):
        self.n[name] = 0
        self.sem_names.append(name)

    def op(self, eng, f, waits=(), inc=None, inc_n=1):
        n_after = None
        if inc is not None:
            self.n[inc] += inc_n
            n_after = self.n[inc]
        waits = tuple((s, v) for (s, v) in waits if v)

        def g(EE, sems, _f=f, _w=waits, _inc=inc, _nn=inc_n):
            for s, v in _w:
                EE.wait_ge(sems[s], v)
            ins = _f(EE)
            if _inc is not None:
                ins.then_inc(sems[_inc], _nn)
        self.ops[eng].append(g)
        return n_after

    def wait(self, eng, waits):
        waits = tuple((s, v) for (s, v) in waits if v)
        if not waits:
            return

        def g(EE, sems, _w=waits):
            for s, v in _w:
                EE.wait_ge(sems[s], v)
        self.ops[eng].append(g)

    SYNC_DMA_SEMS = ("sIO", "sLD0", "sLD1", "sKV0", "sKV1", "sVT",
                     "sQS0", "sQS1", "sOS", "sOM0", "sOM1")

    def dma(self, eng, out, in_, waits=(), sem=None):
        sem = sem or ("sW0" if eng == "gpsimd" else "sIO")
        return self.op(eng, lambda EE, _o=out, _i=in_: EE.dma_start(out=_o, in_=_i),
                       waits, sem, 16)

    def flush_sync(self):
        self.wait("sync", [(sm, self.n[sm]) for sm in Prog.SYNC_DMA_SEMS])

    def join(self):
        vals = {}
        for e in ENGS:
            js = "j_" + e
            self.n[js] += 1
            vals[e] = (js, self.n[js])
        for e in ENGS:
            js, v = vals[e]
            others = [vals[o] for o in ENGS if o != e]

            def g(EE, sems, _js=js, _o=others):
                EE.drain().then_inc(sems[_js], 1)
                for s, vv in _o:
                    EE.wait_ge(sems[s], vv)
            self.ops[e].append(g)


def build_program():
    nc = bass.Bass("TRN2", target_bir_lowering=False)
    p = Prog()
    for s in ("sW0", "sW1", "sIO", "sPE", "sV", "sS", "sCC",
              "sLD0", "sLD1", "sKV0", "sKV1", "sVT", "sQS0", "sQS1",
              "sOS", "sOM0", "sOM1"):
        p.sem(s)
    for e in ENGS:
        p.sem("j_" + e)
    N = p.n

    din = {}

    def dram_in(name, shape, dt):
        din[name] = nc.dram_tensor(name, list(shape), dt, kind="ExternalInput").ap()
        return din[name]

    x_in = dram_in("x", [D, T], F32)
    for pre in ("a0", "a1"):
        for w in ("qT", "kT", "vT", "oT"):
            dram_in(pre + w, [D, D], F32)
    for bn in ("bq", "bk", "bv", "bo"):
        dram_in("a0" + bn, [128, 16], F32)
    dram_in("w1", [E, D, FF], F32)
    dram_in("w2", [E, D, FF], F32)
    dram_in("w3", [E, FF, D], F32)
    dram_in("rwT", [D, E], F32)
    dram_in("cI_bf", [128, 128], BF16)
    dram_in("cI_f32", [128, 128], F32)
    dram_in("cones_col", [128, 1], F32)
    dram_in("cones_row", [1, 128], F32)
    dram_in("cmaskA", [2, 128, 128], BF16)
    dram_in("cmaskBd", [2, 128, 128], BF16)
    dram_in("ciota", [128, 8], F32)
    dram_in("ciotaB", [128, 8], F32)

    out_t = nc.dram_tensor("out", [D, T], F32, kind="ExternalOutput").ap()
    dbg = {}
    if DBG:
        for nm, shape, dt in (("dbg_kvin0", [16384, 512], BF16),
                              ("dbg_kvout0", [32768, 512], BF16),
                              ("dbg_q0", [D, T], BF16),
                              ("dbg_o0", [D, T], BF16),
                              ("dbg_r0", [D, T], F32),
                              ("dbg_r1", [D, T], F32),
                              ("dbg_o1", [D, T], BF16),
                              ("dbg_r2", [D, T], F32),
                              ("dbg_ml", [128, 48], F32)):
            dbg[nm] = nc.dram_tensor(nm, shape, dt, kind="ExternalOutput").ap()

    q_st = nc.dram_tensor("q_st", [D, T], BF16, kind="Internal").ap()
    o_st = nc.dram_tensor("o_st", [D, T], BF16, kind="Internal").ap()
    kv_in = [nc.dram_tensor(f"kv_in{a}", [16384, 512], BF16, kind="Internal").ap()
             for a in range(2)]
    kv_out = [nc.dram_tensor(f"kv_out{a}", [32768, 512], BF16, kind="Internal").ap()
              for a in range(2)]

    def kv_in_K(a, h):
        v = kv_in[a][0:8192, :].rearrange("(a b) c -> a (b c)", b=4)
        return v[h * 128:(h + 1) * 128, :]

    def kv_in_V(a, h, u0, u1):
        v = kv_in[a][8192:16384, :].rearrange("a (b c) -> (a b) c", b=4, c=128)
        return v[h * 2048 + u0: h * 2048 + u1, :]

    def kv_out_K(a, r, h):
        v = kv_out[a][r * 16384: r * 16384 + 8192, :].rearrange("(a b) c -> a (b c)", b=4)
        return v[h * 128:(h + 1) * 128, :]

    def kv_out_V(a, r, h):
        v = kv_out[a][r * 16384 + 8192:(r + 1) * 16384, :].rearrange(
            "a (b c) -> (a b) c", b=4, c=128)
        return v[h * 2048:(h + 1) * 2048, :]

    with ExitStack() as stk:
        en = stk.enter_context
        resid = en(nc.sbuf_tensor("resid", [128, DC * 2048], F32))
        AR = 13312
        arena = en(nc.sbuf_tensor("arena", [128, AR], F32))
        ws0 = en(nc.sbuf_tensor("ws0", [128, 4096], BF16))
        ws1 = en(nc.sbuf_tensor("ws1", [128, 4096], BF16))
        WS = [ws0, ws1]
        ident_bf = en(nc.sbuf_tensor("ident_bf", [128, 128], BF16))
        ident_f = en(nc.sbuf_tensor("ident_f", [128, 128], F32))
        ones_col = en(nc.sbuf_tensor("ones_col", [128, 1], F32))
        ones_row = en(nc.sbuf_tensor("ones_row", [1, 128], F32))
        maskA = en(nc.sbuf_tensor("maskA", [128, 256], BF16))
        maskBd = en(nc.sbuf_tensor("maskBd", [128, 256], BF16))
        iota_sb = en(nc.sbuf_tensor("iota_sb", [128, 8], F32))
        iotaB_sb = en(nc.sbuf_tensor("iotaB_sb", [128, 8], F32))
        bias_sb = en(nc.sbuf_tensor("bias_sb", [128, 64], F32))
        rw_sb = en(nc.sbuf_tensor("rw_sb", [128, 128], F32))
        Srow_sb = en(nc.sbuf_tensor("Srow_sb", [128, 8], F32))
        m_cols = en(nc.sbuf_tensor("m_cols", [128, 16], F32))
        l_cols = en(nc.sbuf_tensor("l_cols", [128, 16], F32))
        rcp_cols = en(nc.sbuf_tensor("rcp_cols", [128, 16], F32))
        tmp_c = en(nc.sbuf_tensor("tmp_c", [128, 8], F32))
        lgt = en(nc.sbuf_tensor("lgt", [128, 40], F32))
        g_sb = en(nc.sbuf_tensor("g_sb", [128, 128], F32))
        row_a = en(nc.sbuf_tensor("row_a", [1, 1024], F32))
        pz = en(nc.psum_tensor("pz", [128, 4096], F32))

        sems = {s: en(nc.semaphore(s)) for s in p.sem_names}

        def af(o, n):
            return arena[:, o:o + n]

        def ab(o, nbf):
            return arena[:, o:o + (nbf + 1) // 2].bitcast(BF16)

        def arow(o, n):
            return arena[0:1, o:o + n]

        def pzf(o, n):
            return pz[:, o:o + n]

        def pzb(o, nbf):
            return pz[:, o:o + (nbf + 1) // 2].bitcast(BF16)

        LN_XLN = 0
        LN_A = 8192
        LN_B = 9216
        LN_T1 = 10240
        LN_SQ = 11264
        LN_RC = 11264
        LN_RD = 11776
        PRJ_S0 = 12288
        PRJ_S1 = 12800

        def xln(ic, t0, t1):
            return ab(LN_XLN, 16384)[:, ic * 1024 + t0: ic * 1024 + t1]

        C_K = [0, 5120]
        C_V = [2048, 7168]
        C_Q = [4096, 9216]
        C_PT = [10240, 10496]
        C_MROW = 10752
        C_LROW = 11264
        C_RLB = 11776
        C_OST = 12288
        C_FREE = 12544

        D_WO = 0
        D_OM = [8192, 8448]

        F_HS = 8192
        F_GATE = 12288
        F_SG = 12800
        F_SIL = 13056

        def hs_sl(fc, tb):
            return ab(F_HS, 8192)[:, (fc * 2 + tb) * 512:(fc * 2 + tb) * 512 + 512]

        def t_mm(out, lh, rh, start, stop, waits=(), inc=None):
            return p.op("tensor",
                        lambda EE, _o=out, _l=lh, _r=rh, _s=start, _p=stop:
                        EE.matmul(_o, _l, _r, start=_s, stop=_p), waits, inc)

        def t_tr(out, in_, idt, waits=(), inc=None):
            return p.op("tensor",
                        lambda EE, _o=out, _i=in_, _d=idt:
                        EE.transpose(_o, _i, _d), waits, inc)

        def v_op(f, waits=(), inc="sV"):
            return p.op("vector", f, waits, inc)

        def s_op(f, waits=(), inc="sS"):
            return p.op("scalar", f, waits, inc)

        p.dma("sync", ident_bf[:, :], din["cI_bf"][:, :])
        p.dma("sync", ident_f[:, :], din["cI_f32"][:, :])
        p.dma("sync", ones_col[:, :], din["cones_col"][:, :])
        p.dma("sync", ones_row[:, :], din["cones_row"][:, :])
        p.dma("sync", maskA[:, :].rearrange("a (r b) -> a r b", r=2),
              din["cmaskA"].rearrange("r a b -> a r b"))
        p.dma("sync", maskBd[:, :].rearrange("a (r b) -> a r b", r=2),
              din["cmaskBd"].rearrange("r a b -> a r b"))
        p.dma("sync", iota_sb[:, :], din["ciota"][:, :])
        p.dma("sync", iotaB_sb[:, :], din["ciotaB"][:, :])
        for i, bn in enumerate(("bq", "bk", "bv", "bo")):
            p.dma("sync", bias_sb[:, i * 16:(i + 1) * 16], din["a0" + bn][:, :])
        p.dma("sync", rw_sb[:, :].rearrange("p (ic e) -> p ic e", e=8),
              din["rwT"].rearrange("(ic p) e -> p ic e", p=128))
        for c in range(DC):
            p.dma("sync", resid[:, c * 2048:(c + 1) * 2048], x_in[c * 128:(c + 1) * 128, :])
        consts_v = N["sIO"]
        for e in ("tensor", "vector", "scalar", "gpsimd"):
            p.wait(e, [("sIO", consts_v)])

        def emit_ln(th, guard_v=None, guard_s=None):
            t0 = th * TH
            sum_ps = [pz[0:1, 0:512], pz[0:1, 512:1024]]
            sq_ps = [pz[0:1, 1024:1536], pz[0:1, 1536:2048]]
            sq_sb = [af(LN_SQ, 512), af(LN_SQ + 512, 512)]
            sq_mm = {}
            pe_stats = None
            for blk in range(2):
                for ic in range(DC):
                    k = blk * DC + ic
                    xs = resid[:, ic * 2048 + t0 + blk * 512: ic * 2048 + t0 + blk * 512 + 512]
                    w0 = []
                    if blk == 0 and ic == 0 and guard_v:
                        w0 = [("sV", guard_v)]
                    t_mm(sum_ps[blk], ones_col[:, :], xs, start=(ic == 0),
                         stop=(ic == DC - 1), waits=w0)
                    ws_ = [("sPE", sq_mm[k - 2])] if k >= 2 else (
                        [("sS", guard_s)] if (k == 0 and guard_s) else [])
                    sv = s_op(lambda EE, _o=sq_sb[k % 2], _i=xs:
                              EE.activation(_o, _i, AF.Square), ws_)
                    sq_mm[k] = t_mm(sq_ps[blk], ones_col[:, :], sq_sb[k % 2],
                                    start=(ic == 0), stop=(ic == DC - 1),
                                    waits=[("sS", sv)], inc="sPE")
            pe_stats = sq_mm[2 * DC - 1]
            rc = arow(11264, 1024)
            rd = arow(0, 1024)
            row_b = arow(12288, 1024)
            c1 = 1.0 / float(D)
            vv = None
            for blk in range(2):
                sl = slice(blk * 512, blk * 512 + 512)
                na = v_op(lambda EE, _o=row_a[0:1, sl], _i=sum_ps[blk]:
                          EE.tensor_scalar_mul(_o, _i, c1),
                          [("sPE", pe_stats)] if blk == 0 else [])
                nb_ = v_op(lambda EE, _o=rc[0:1, sl], _i=sq_ps[blk]:
                           EE.tensor_scalar_mul(_o, _i, c1))
                nd = v_op(lambda EE, _o=rd[0:1, sl], _i=row_a[0:1, sl]:
                          EE.tensor_mul(_o, _i, _i), [("sV", na)])
                vv = v_op(lambda EE, _o=rc[0:1, sl], _a=rc[0:1, sl], _b=rd[0:1, sl]:
                          EE.tensor_sub(_o, _a, _b), [("sV", nd)])
            vv = v_op(lambda EE: EE.tensor_scalar_add(rc, rc, EPS), [("sV", vv)])
            ss = s_op(lambda EE: EE.activation(rc, rc, AF.Sqrt), [("sV", vv)])
            vrs = v_op(lambda EE: EE.reciprocal(row_b[0:1, :], rc), [("sS", ss)])
            nm = v_op(lambda EE: EE.tensor_mul(rd, row_a[0:1, :], row_b[0:1, :]),
                      [("sV", vrs)])
            vbr = v_op(lambda EE: EE.tensor_scalar_mul(rd, rd, -1.0), [("sV", nm)])
            pe_b = None
            for blk in range(2):
                sl = slice(blk * 512, blk * 512 + 512)
                t_mm(pzf(2048 + blk * 512, 512), ones_row[0:1, :], row_b[0:1, sl],
                     start=True, stop=True, waits=[("sV", vbr)] if blk == 0 else [])
                pe_b = t_mm(pzf(3072 + blk * 512, 512), ones_row[0:1, :], rd[0:1, sl],
                            start=True, stop=True, inc="sPE")
            vA = v_op(lambda EE: EE.tensor_copy(af(LN_A, 1024), pzf(2048, 1024)),
                      [("sPE", pe_b)])
            vB = v_op(lambda EE: EE.tensor_copy(af(LN_B, 1024), pzf(3072, 1024)))
            last_v = None
            for ic in range(DC):
                xs = resid[:, ic * 2048 + t0: ic * 2048 + t0 + TH]
                nmu = v_op(lambda EE, _i=xs:
                           EE.tensor_mul(af(LN_T1, 1024), _i, af(LN_A, 1024)),
                           [("sV", vB)] if ic == 0 else [])
                last_v = v_op(lambda EE, _o=xln(ic, 0, TH):
                              EE.tensor_add(_o, af(LN_T1, 1024), af(LN_B, 1024)),
                              [("sV", nmu)])
            return last_v

        prj = {"ws_pe": [None, None], "ev": [None, None], "stg_dma": [None, None],
               "vt_dma": None, "vt_v": {}}

        def emit_proj(a, wname, kind, th, bias_i, v_xln):
            t0 = th * TH
            wT = din[wname].rearrange("(ic p) o -> p ic o", p=128)
            stg = [ab(PRJ_S0, 1024), ab(PRJ_S1, 1024)]
            for oc in range(DC):
                buf = oc % 2
                wv = p.dma("gpsimd",
                           WS[buf][:, 0:2048].rearrange("p (ic o) -> p ic o", o=128),
                           wT[:, :, oc * 128:(oc + 1) * 128],
                           [("sPE", prj["ws_pe"][buf])] if prj["ws_pe"][buf] else [],
                           sem=f"sW{buf}")
                ppsum = pzf(buf * 1024, 1024)
                mm_w = [(f"sW{buf}", wv), ("sV", v_xln)]
                if prj["ev"][buf]:
                    mm_w.append(("sS", prj["ev"][buf]))
                pe_v = None
                for ic in range(DC):
                    for tb in range(2):
                        pe_v = t_mm(ppsum[:, tb * 512:tb * 512 + 512],
                                    WS[buf][:, ic * 128:(ic + 1) * 128],
                                    xln(ic, tb * 512, tb * 512 + 512),
                                    start=(ic == 0), stop=(ic == DC - 1),
                                    waits=mm_w if (ic == 0 and tb == 0) else (),
                                    inc="sPE" if (ic == DC - 1 and tb == 1) else None)
                prj["ws_pe"][buf] = pe_v
                w_e = [("sPE", pe_v)]
                if prj["stg_dma"][buf]:
                    w_e.append(prj["stg_dma"][buf])
                if bias_i is not None:
                    bcol = bias_sb[:, bias_i * 16 + oc: bias_i * 16 + oc + 1]
                    ev = s_op(lambda EE, _o=stg[buf], _i=ppsum, _b=bcol:
                              EE.activation(_o, _i, AF.Identity, bias=_b), w_e)
                else:
                    ev = s_op(lambda EE, _o=stg[buf], _i=ppsum:
                              EE.activation(_o, _i, AF.Identity), w_e)
                prj["ev"][buf] = ev
                h = oc
                if kind == "k":
                    prj["stg_dma"][buf] = (f"sKV{buf}", p.dma(
                        "sync", kv_in_K(a, h)[:, t0:t0 + TH], stg[buf], [("sS", ev)],
                        sem=f"sKV{buf}"))
                elif kind == "q":
                    prj["stg_dma"][buf] = (f"sQS{buf}", p.dma(
                        "sync", q_st[h * 128:(h + 1) * 128, t0:t0 + TH], stg[buf],
                        [("sS", ev)], sem=f"sQS{buf}"))
                else:
                    vt = ab(LN_T1, 2048)
                    vt = ab(LN_T1, 1024)
                    for uc in range(8):
                        slot = uc % 2
                        ptr = pzb(2048 + slot * 512, 256)
                        wtr = [("sS", ev)] if uc == 0 else []
                        g = prj["vt_v"].get(slot)
                        if g:
                            wtr.append(("sV", g))
                        if uc == 0 and prj["vt_dma"]:
                            wtr.append(("sVT", prj["vt_dma"]))
                        tv = t_tr(ptr[:, 0:128], stg[buf][:, uc * 128:(uc + 1) * 128],
                                  ident_bf[:, :], wtr, "sPE")
                        prj["vt_v"][slot] = v_op(
                            lambda EE, _o=vt[:, uc * 128:(uc + 1) * 128], _i=ptr[:, 0:128]:
                            EE.tensor_copy(_o, _i), [("sPE", tv)])
                    dst = kv_in_V(a, h, t0, t0 + TH).rearrange("(uc up) d -> up uc d", up=128)
                    prj["vt_dma"] = p.dma(
                        "sync", dst, vt.rearrange("p (uc d) -> p uc d", d=128),
                        [("sV", prj["vt_v"][1])], sem="sVT")

        def emit_flash(a):
            for h in range(H if H_LIM is None else H_LIM):
                buf = h % 2
                ksb = lambda r, _b=buf: ab(C_K[_b] + r * 1024, 2048)
                vsb = lambda r, _b=buf: ab(C_V[_b] + r * 1024, 2048)
                qsb = ab(C_Q[buf], 2048)
                if h == 0:
                    ld = emit_loads(a, 0)
                    p.cur_ld = ld
                ld = p.cur_ld
                if h + 1 < (H if H_LIM is None else H_LIM):
                    p.cur_ld = emit_loads(a, h + 1)

                exp_done = [None, None]
                last_ev = [None]
                for i in range(16 if I_LIM is None else I_LIM):
                    kmax = 128 * (i + 1)
                    qstat = qsb[:, i * 128:(i + 1) * 128]
                    nseg = (kmax + 1023) // 1024
                    first_blk = True
                    for g in range(nseg):
                        u0 = g * 1024
                        w = min(1024, kmax - u0)
                        final = (u0 + w == kmax)
                        for r in range(2):
                            reg = pzf(r * 1024, 1024)
                            nblk = (w + 511) // 512
                            pe_v = None
                            for bk in range(nblk):
                                bw = min(512, w - bk * 512)
                                wts = []
                                if bk == 0:
                                    if i == 0 and g == 0 and r == 0:
                                        wts.append((f"sLD{buf}", ld))
                                    if exp_done[r]:
                                        wts.append(exp_done[r])
                                last_data = (bk == nblk - 1)
                                if not PA_MM:
                                    pe_v = p.op("tensor", lambda EE: EE.nop(), wts,
                                                "sPE" if (last_data and not final) else None)
                                    continue
                                pe_v = t_mm(reg[:, bk * 512:bk * 512 + bw], qstat,
                                            ksb(r)[:, u0 + bk * 512: u0 + bk * 512 + bw],
                                            start=True,
                                            stop=(last_data and not final),
                                            waits=wts,
                                            inc="sPE" if (last_data and not final) else None)
                            if not PA_MM:
                                if final:
                                    pe_v = p.op("tensor", lambda EE: EE.nop(), (), "sPE")
                            elif final and PA_MASK:
                                pe_v = t_mm(reg[:, w - 128:w], ident_bf[:, :],
                                            maskA[:, r * 128:(r + 1) * 128],
                                            start=False, stop=True, inc="sPE")
                            elif final:
                                pe_v = t_mm(reg[:, 0:w], qstat,
                                            ksb(r)[:, u0:u0 + w],
                                            start=True, stop=True, inc="sPE")
                            if not PA_RED:
                                v_op(lambda EE: EE.tensor_copy(tmp_c[:, 0:1], tmp_c[:, 1:2]),
                                     [("sPE", pe_v)])
                                first_blk = False
                                continue
                            if first_blk:
                                vv = v_op(lambda EE, _i=reg[:, 0:w], _m=m_cols[:, i:i + 1]:
                                          EE.reduce_max(_m, _i, axis=AX.X),
                                          [("sPE", pe_v)])
                            else:
                                n1 = v_op(lambda EE, _i=reg[:, 0:w]:
                                          EE.reduce_max(tmp_c[:, 0:1], _i, axis=AX.X),
                                          [("sPE", pe_v)])
                                v_op(lambda EE, _m=m_cols[:, i:i + 1]:
                                     EE.tensor_scalar(tmp_c[:, 1:2], _m, tmp_c[:, 0:1],
                                                      0.0, ALU.subtract, ALU.min),
                                     [("sV", n1)])
                                vv = v_op(lambda EE, _m=m_cols[:, i:i + 1]:
                                          EE.tensor_max(_m, _m, tmp_c[:, 0:1]))
                            w_sub = [("sV", vv)]
                            if last_ev[0]:
                                w_sub.append(("sS", last_ev[0]))
                            sub = v_op(lambda EE, _i=reg[:, 0:w], _w=w,
                                       _m=m_cols[:, i:i + 1]:
                                       EE.tensor_scalar(pzb_discard[:, 0:_w], _i, _m,
                                                        None, ALU.subtract),
                                       w_sub)
                            exp_done[r] = ("sV", sub)
                            if not first_blk:
                                s_op(lambda EE: EE.activation(tmp_c[:, 3:4], tmp_c[:, 1:2],
                                                              AF.Exp, scale=SCL),
                                     [("sV", vv)])
                            ev = s_op(lambda EE, _w=w:
                                      EE.activation(exp_disc[:, 0:_w], pzb_discard[:, 0:_w],
                                                    AF.Exp, scale=SCL,
                                                    accum_out=tmp_c[:, 4:5]),
                                      [("sV", sub)])
                            last_ev[0] = ev
                            if first_blk:
                                v_op(lambda EE, _l=l_cols[:, i:i + 1]:
                                     EE.tensor_copy(_l, tmp_c[:, 4:5]), [("sS", ev)])
                            else:
                                v_op(lambda EE, _l=l_cols[:, i:i + 1]:
                                     EE.scalar_tensor_tensor(_l, _l, tmp_c[:, 3:4],
                                                             tmp_c[:, 4:5],
                                                             ALU.mult, ALU.add),
                                     [("sS", ev)])
                            first_blk = False
                v_pa = N["sV"]
                v_rcp = v_op(lambda EE: EE.reciprocal(rcp_cols[:, :], l_cols[:, :]),
                             [("sV", v_pa)])
                if not PASSB:
                    p.join()
                    continue

                st_regs = [pzf(2048, 512), pzf(2560, 512)]
                st_guard = [[exp_done[0]] if exp_done[0] else [],
                            [exp_done[1]] if exp_done[1] else []]
                pts = [ab(C_PT[0], 512), ab(C_PT[1], 512)]
                pt_guard = [None, None]
                o_reg = pzf(3072, 512)
                prow = pz[0:1, 3584:4096]
                o_guard = None
                prow_guard = []
                step = 0
                for J in range(4):
                    trs = None
                    for t in range(4):
                        i = 4 * J + t
                        wtr = list(prow_guard) if t == 0 else []
                        if J == 0 and t == 0:
                            wtr.append(("sV", v_rcp))
                        trs = t_tr(prow[0:1, t * 128:(t + 1) * 128], m_cols[:, i:i + 1],
                                   ident_f[:, :], wtr, "sPE")
                    vmr = v_op(lambda EE: EE.tensor_scalar_mul(arow(C_MROW, 512),
                                                               prow[0:1, :], -1.0),
                               [("sPE", trs)])
                    for t in range(4):
                        i = 4 * J + t
                        trs = t_tr(prow[0:1, t * 128:(t + 1) * 128], rcp_cols[:, i:i + 1],
                                   ident_f[:, :], [("sV", vmr)] if t == 0 else [], "sPE")
                    vlr = v_op(lambda EE: EE.tensor_copy(arow(C_LROW, 512), prow[0:1, :]),
                               [("sPE", trs)])
                    prow_guard = [("sV", vlr)]

                    qmov = qsb[:, J * 512:(J + 1) * 512]
                    first_av = True
                    av_pe = None
                    for r in range(2):
                        for m in range(4 * J + 4):
                            sreg = st_regs[step % 2]
                            pt = pts[step % 2]
                            d = m - 4 * J
                            q0 = 128 * d if d > 0 else 0
                            wN = 512 - q0
                            w0 = [("sV", vmr)] + st_guard[step % 2]
                            t_mm(sreg[:, q0:512], ones_row[0:1, :],
                                 arow(C_MROW, 512)[0:1, q0:512],
                                 start=True, stop=False, waits=w0)
                            if d >= 0:
                                t_mm(sreg[:, q0:512],
                                     ksb(r)[:, m * 128:(m + 1) * 128],
                                     qmov[:, q0:512], start=False, stop=False)
                                pe_v = t_mm(sreg[:, q0:q0 + 128], ident_bf[:, :],
                                            maskBd[:, r * 128:(r + 1) * 128],
                                            start=False, stop=True, inc="sPE")
                            else:
                                pe_v = t_mm(sreg[:, 0:512],
                                            ksb(r)[:, m * 128:(m + 1) * 128],
                                            qmov, start=False, stop=True, inc="sPE")
                            w1 = [("sPE", pe_v)]
                            if pt_guard[step % 2]:
                                w1.append(("sPE", pt_guard[step % 2]))
                            ev = s_op(lambda EE, _o=pt[:, q0:512], _i=sreg[:, q0:512]:
                                      EE.activation(_o, _i, AF.Exp, scale=SCL), w1)
                            st_guard[step % 2] = [("sS", ev)]
                            wav = [("sS", ev)]
                            if first_av and o_guard:
                                wav.append(("sV", o_guard))
                            avv = t_mm(o_reg[:, q0:512],
                                       vsb(r)[:, m * 128:(m + 1) * 128],
                                       pt[:, q0:512],
                                       start=first_av, stop=False,
                                       waits=wav, inc="sPE")
                            pt_guard[step % 2] = avv
                            av_pe = avv
                            first_av = False
                            step += 1
                    bcv = t_mm(st_regs[0], ones_row[0:1, :], arow(C_LROW, 512)[0:1, :],
                               start=True, stop=True,
                               waits=[("sV", vlr)] + st_guard[0], inc="sPE")
                    vb = v_op(lambda EE: EE.tensor_copy(af(C_RLB, 512), st_regs[0]),
                              [("sPE", bcv)])
                    st_guard[0] = [("sV", vb)]
                    w_o = [("sPE", av_pe), ("sV", vb)]
                    if p.ost_dma:
                        w_o.append(("sOS", p.ost_dma))
                    ov = v_op(lambda EE: EE.tensor_tensor(ab(C_OST, 512), o_reg,
                                                          af(C_RLB, 512), ALU.mult), w_o)
                    o_guard = ov
                    p.ost_dma = p.dma("sync",
                                      o_st[h * 128:(h + 1) * 128, J * 512:(J + 1) * 512],
                                      ab(C_OST, 512), [("sV", ov)], sem="sOS")
                p.join()

        def emit_loads(a, h):
            buf = h % 2
            sm = f"sLD{buf}"
            for r in range(2):
                p.dma("sync", ab(C_K[buf] + r * 1024, 2048), kv_out_K(a, r, h), sem=sm)
                p.dma("sync",
                      ab(C_V[buf] + r * 1024, 2048).rearrange("p (uc d) -> p uc d", d=128),
                      kv_out_V(a, r, h).rearrange("(uc up) d -> up uc d", up=128), sem=sm)
            return p.dma("sync", ab(C_Q[buf], 2048), q_st[h * 128:(h + 1) * 128, :],
                         sem=sm)

        def emit_oproj(a, bias_i):
            wT = din[("a0oT" if a == 0 else "a1oT")].rearrange("(hc p) o -> p hc o", p=128)
            omov = [ab(D_OM[0], 512), ab(D_OM[1], 512)]
            om_pe = [None, None]
            y_ev = [None] * 8
            wload_pe = None
            for ohalf in range(2):
                wv = None
                for hc in range(16):
                    wv = p.dma("gpsimd", ab(D_WO + hc * 512, 1024),
                               wT[:, hc, ohalf * 1024:(ohalf + 1) * 1024],
                               [("sPE", wload_pe)] if (hc == 0 and wload_pe) else [])
                for tq in range(4):
                    mm_done = None
                    for hc in range(16):
                        buf = hc % 2
                        mv = p.dma("sync", omov[buf],
                                   o_st[hc * 128:(hc + 1) * 128, tq * 512:(tq + 1) * 512],
                                   [("sPE", om_pe[buf])] if om_pe[buf] else [],
                                   sem=f"sOM{buf}")
                        for oi in range(8):
                            w_mm = []
                            if oi == 0:
                                w_mm.append((f"sOM{buf}", mv))
                                if hc == 0 and tq == 0:
                                    w_mm.append(("sW0", wv))
                            if hc == 0 and y_ev[oi]:
                                w_mm.append(("sV", y_ev[oi]))
                            mm_done = t_mm(pzf(oi * 512, 512),
                                           ab(D_WO + hc * 512, 1024)[:, oi * 128:(oi + 1) * 128],
                                           omov[buf],
                                           start=(hc == 0), stop=(hc == 15),
                                           waits=w_mm,
                                           inc="sPE" if oi == 7 else None)
                        om_pe[buf] = mm_done
                    for oi in range(8):
                        oc = ohalf * 8 + oi
                        rsl = resid[:, oc * 2048 + tq * 512: oc * 2048 + tq * 512 + 512]
                        if bias_i is not None:
                            bcol = bias_sb[:, bias_i * 16 + oc: bias_i * 16 + oc + 1]
                            y_ev[oi] = v_op(
                                lambda EE, _r=rsl, _p=pzf(oi * 512, 512), _b=bcol:
                                EE.scalar_tensor_tensor(_r, _p, _b, _r, ALU.add, ALU.add),
                                [("sPE", mm_done)] if oi == 0 else [])
                        else:
                            y_ev[oi] = v_op(
                                lambda EE, _r=rsl, _p=pzf(oi * 512, 512):
                                EE.tensor_tensor(_r, _p, _r, ALU.add),
                                [("sPE", mm_done)] if oi == 0 else [])
                wload_pe = om_pe[1]
            p.join()

        def emit_router(th, v_xln):
            t0 = th * TH
            for ic in range(DC):
                t_mm(pz[0:1, 1024:1032], ones_col[:, :], rw_sb[:, ic * 8:(ic + 1) * 8],
                     start=(ic == 0), stop=(ic == DC - 1),
                     waits=[("sV", v_xln)] if ic == 0 else [],
                     inc="sPE" if ic == DC - 1 else None)
            pe_s = N["sPE"]
            vs1 = v_op(lambda EE: EE.tensor_copy(lgt[:, 0:8][0:1, :], pz[0:1, 1024:1032]),
                       [("sPE", pe_s)])
            bs = t_mm(pzf(1536, 8), ones_row[0:1, :], lgt[0:1, 0:8],
                      start=True, stop=True, waits=[("sV", vs1)], inc="sPE")
            vsb_ = v_op(lambda EE: EE.tensor_copy(Srow_sb[:, :], pzf(1536, 8)),
                        [("sPE", bs)])
            rt_guard = [None, None]
            last_v = vsb_
            for tt in range(8):
                bank = tt % 2
                lg_ps = pzf(bank * 512, 8)
                acol = pzf(bank * 512 + 128, 128)
                bcol = pzf(bank * 512 + 256, 128)
                for ic in range(DC):
                    w0 = []
                    if ic == 0:
                        w0 = [("sV", last_v)] if tt == 0 else (
                            [("sV", rt_guard[bank])] if rt_guard[bank] else [])
                    t_mm(lg_ps,
                         resid[:, ic * 2048 + t0 + tt * 128: ic * 2048 + t0 + tt * 128 + 128],
                         rw_sb[:, ic * 8:(ic + 1) * 8],
                         start=(ic == 0), stop=(ic == DC - 1), waits=w0)
                t_tr(acol, af(LN_A, 1024)[:, tt * 128:(tt + 1) * 128], ident_f[:, :])
                pe_v = t_tr(bcol, af(LN_B, 1024)[:, tt * 128:(tt + 1) * 128],
                            ident_f[:, :], inc="sPE")
                L = lgt
                n = v_op(lambda EE, _p=lg_ps: EE.tensor_copy(L[:, 0:8], _p),
                         [("sPE", pe_v)])
                n = v_op(lambda EE, _p=acol: EE.tensor_copy(tmp_c[:, 0:1], _p[:, 0:1]))
                vq = v_op(lambda EE, _p=bcol: EE.tensor_copy(tmp_c[:, 1:2], _p[:, 0:1]))
                rt_guard[bank] = vq
                n = v_op(lambda EE: EE.tensor_scalar(L[:, 8:16], Srow_sb[:, :],
                                                     tmp_c[:, 1:2], None, ALU.mult),
                         [("sV", vq)])
                n = v_op(lambda EE: EE.tensor_scalar(L[:, 0:8], L[:, 0:8],
                                                     tmp_c[:, 0:1], None, ALU.mult))
                n = v_op(lambda EE: EE.tensor_add(L[:, 0:8], L[:, 0:8], L[:, 8:16]),
                         [("sV", n)])
                n = v_op(lambda EE: EE.reduce_max(tmp_c[:, 2:3], L[:, 0:8], axis=AX.X),
                         [("sV", n)])
                n = v_op(lambda EE: EE.tensor_scalar(L[:, 16:24], L[:, 0:8],
                                                     tmp_c[:, 2:3], None, ALU.is_ge),
                         [("sV", n)])
                n = v_op(lambda EE: EE.scalar_tensor_tensor(L[:, 8:16], L[:, 16:24], -BIG,
                                                            iotaB_sb[:, :], ALU.mult,
                                                            ALU.add), [("sV", n)])
                n = v_op(lambda EE: EE.tensor_reduce(tmp_c[:, 3:4], L[:, 8:16], AX.X,
                                                     ALU.min), [("sV", n)])
                n = v_op(lambda EE: EE.tensor_scalar(L[:, 16:24], iota_sb[:, :],
                                                     tmp_c[:, 3:4], None, ALU.is_equal),
                         [("sV", n)])
                n = v_op(lambda EE: EE.scalar_tensor_tensor(L[:, 24:32], L[:, 16:24], NEG,
                                                            L[:, 0:8], ALU.mult, ALU.add),
                         [("sV", n)])
                n = v_op(lambda EE: EE.reduce_max(tmp_c[:, 4:5], L[:, 24:32], axis=AX.X),
                         [("sV", n)])
                n = v_op(lambda EE: EE.tensor_scalar(L[:, 32:40], L[:, 24:32],
                                                     tmp_c[:, 4:5], None, ALU.is_ge),
                         [("sV", n)])
                n = v_op(lambda EE: EE.scalar_tensor_tensor(L[:, 8:16], L[:, 32:40], -BIG,
                                                            iotaB_sb[:, :], ALU.mult,
                                                            ALU.add), [("sV", n)])
                n = v_op(lambda EE: EE.tensor_reduce(tmp_c[:, 5:6], L[:, 8:16], AX.X,
                                                     ALU.min), [("sV", n)])
                n = v_op(lambda EE: EE.tensor_scalar(L[:, 32:40], iota_sb[:, :],
                                                     tmp_c[:, 5:6], None, ALU.is_equal),
                         [("sV", n)])
                vd = v_op(lambda EE: EE.tensor_sub(tmp_c[:, 6:7], tmp_c[:, 2:3],
                                                   tmp_c[:, 4:5]), [("sV", n)])
                s1 = s_op(lambda EE: EE.activation(tmp_c[:, 7:8], tmp_c[:, 6:7],
                                                   AF.Sigmoid), [("sV", vd)])
                vn = v_op(lambda EE: EE.tensor_scalar_mul(tmp_c[:, 6:7],
                                                          tmp_c[:, 6:7], -1.0),
                          [("sS", s1)])
                s2 = s_op(lambda EE: EE.activation(tmp_c[:, 6:7], tmp_c[:, 6:7],
                                                   AF.Sigmoid), [("sV", vn)])
                n = v_op(lambda EE: EE.tensor_scalar(L[:, 16:24], L[:, 16:24],
                                                     tmp_c[:, 7:8], None, ALU.mult),
                         [("sS", s1)])
                last_v = v_op(lambda EE, _g=g_sb[:, tt * 8:(tt + 1) * 8]:
                              EE.scalar_tensor_tensor(_g, L[:, 32:40], tmp_c[:, 6:7],
                                                      L[:, 16:24], ALU.mult, ALU.add),
                              [("sS", s2), ("sV", n)])
            return last_v

        def emit_experts(th, v_xln, v_g):
            t0 = th * TH
            gate_b = ab(F_GATE, 1024)
            sg_b = ab(F_SG, 512)
            sil_b = ab(F_SIL, 512)
            w1r = din["w1"].rearrange("e (ic p) f -> e p ic f", p=128)
            w2r = din["w2"].rearrange("e (ic p) f -> e p ic f", p=128)
            w3r = din["w3"].rearrange("e (fc p) d -> e p fc d", p=128)
            ws_guard = [None, None]
            y_guard = [None] * 4
            sil_guard = None
            gate_guard = None
            hs_guard = {}
            for e in range(E):
                gprow = pz[0:1, 2560:3072]
                gate_v = None
                for tb in range(2):
                    trs = None
                    for q4 in range(4):
                        tt = tb * 4 + q4
                        wtr = []
                        if q4 == 0:
                            if e == 0 and tb == 0:
                                wtr.append(("sV", v_g))
                            if gate_guard:
                                wtr.append(("sV", gate_guard))
                            g45 = hs_guard.get((1, 0))
                            if g45:
                                wtr.append(("sV", g45))
                        trs = t_tr(gprow[0:1, q4 * 128:(q4 + 1) * 128],
                                   g_sb[:, tt * 8 + e: tt * 8 + e + 1],
                                   ident_f[:, :], wtr, "sPE")
                    vr = v_op(lambda EE: EE.tensor_copy(arow(C_MROW, 512), gprow[0:1, :]),
                              [("sPE", trs)])
                    w_bc = [("sV", vr)]
                    g45 = hs_guard.get((1, 0))
                    if g45:
                        w_bc.append(("sV", g45))
                    bc = t_mm(pzf(2048, 512), ones_row[0:1, :], arow(C_MROW, 512),
                              start=True, stop=True, waits=w_bc, inc="sPE")
                    gate_v = v_op(lambda EE, _t=tb:
                                  EE.tensor_copy(gate_b[:, _t * 512:(_t + 1) * 512],
                                                 pzf(2048, 512)),
                                  [("sPE", bc)])
                gate_guard = gate_v
                hs_v = None
                for fc in range(FC):
                    par = fc % 2
                    buf = par
                    wv1 = p.dma("gpsimd",
                                WS[buf][:, 0:2048].rearrange("p (ic o) -> p ic o", o=128),
                                w1r[e, :, :, fc * 128:(fc + 1) * 128],
                                [("sPE", ws_guard[buf])] if ws_guard[buf] else [],
                                sem=f"sW{buf}")
                    wv2 = p.dma("gpsimd",
                                WS[buf][:, 2048:4096].rearrange("p (ic o) -> p ic o", o=128),
                                w2r[e, :, :, fc * 128:(fc + 1) * 128], sem=f"sW{buf}")
                    pe_h = {}
                    for mi in range(2):
                        for tb in range(2):
                            reg = pzf((mi * 4 + par * 2 + tb) * 512, 512)
                            w_m = []
                            if tb == 0:
                                w_m = [(f"sW{buf}", wv2), ("sV", v_xln)]
                                if e == 0 and fc == 0 and mi == 0:
                                    w_m.append(("sV", gate_v))
                                g = hs_guard.get((mi, par))
                                if g:
                                    w_m.append(("sV", g))
                                if mi == 0 and y_guard[par * 2]:
                                    w_m.append(("sV", y_guard[par * 2]))
                                if mi == 0 and y_guard[par * 2 + 1]:
                                    w_m.append(("sV", y_guard[par * 2 + 1]))
                                if mi == 1 and e == 0 and fc < 2:
                                    w_m.append(("sV", gate_v))
                            pe_v = None
                            for ic in range(DC):
                                pe_v = t_mm(reg,
                                            WS[buf][:, mi * 2048 + ic * 128:
                                                    mi * 2048 + ic * 128 + 128],
                                            xln(ic, tb * 512, tb * 512 + 512),
                                            start=(ic == 0), stop=(ic == DC - 1),
                                            waits=w_m if ic == 0 else (),
                                            inc="sPE" if ic == DC - 1 else None)
                            pe_h[(mi, tb)] = pe_v
                    ws_guard[buf] = pe_h[(1, 1)]
                    for tb in range(2):
                        h1reg = pzf((par * 2 + tb) * 512, 512)
                        h2reg = pzf((4 + par * 2 + tb) * 512, 512)
                        w_s = [("sPE", pe_h[(0, tb)])]
                        if sil_guard:
                            w_s.append(("sV", sil_guard))
                        sv = s_op(lambda EE, _i=h1reg: EE.activation(sil_b, _i, AF.Silu),
                                  w_s)
                        v1 = v_op(lambda EE, _t=tb:
                                  EE.tensor_tensor(sg_b, sil_b,
                                                   gate_b[:, _t * 512:(_t + 1) * 512],
                                                   ALU.mult),
                                  [("sS", sv), ("sPE", pe_h[(1, tb)])])
                        sil_guard = v1
                        v2 = v_op(lambda EE, _o=hs_sl(fc, tb), _i=h2reg:
                                  EE.tensor_tensor(_o, _i, sg_b, ALU.mult),
                                  [("sV", v1)])
                        hs_guard[(0, par)] = v2
                        hs_guard[(1, par)] = v2
                        hs_v = v2
                for tb in range(2):
                    for dq in range(4):
                        buf = dq % 2
                        wv = None
                        for j in range(4):
                            dc = dq * 4 + j
                            wv = p.dma("gpsimd",
                                       WS[buf][:, j * 1024:(j + 1) * 1024]
                                       .rearrange("p (fc o) -> p fc o", o=128),
                                       w3r[e, :, :, dc * 128:(dc + 1) * 128],
                                       [("sPE", ws_guard[buf])] if (j == 0 and ws_guard[buf]) else [],
                                       sem=f"sW{buf}")
                        mm_j = [None] * 4
                        for j in range(4):
                            dc = dq * 4 + j
                            reg = pzf(j * 512, 512)
                            w_m = []
                            if j == 0:
                                w_m = [(f"sW{buf}", wv), ("sV", hs_v)]
                            if y_guard[j]:
                                w_m.append(("sV", y_guard[j]))
                            if tb == 0 and dq == 0:
                                g = hs_guard.get((0, j // 2))
                                if g:
                                    w_m.append(("sV", g))
                            pe_v = None
                            for fcc in range(FC):
                                pe_v = t_mm(reg,
                                            WS[buf][:, j * 1024 + fcc * 128:
                                                    j * 1024 + fcc * 128 + 128],
                                            hs_sl(fcc, tb),
                                            start=(fcc == 0), stop=(fcc == FC - 1),
                                            waits=w_m if fcc == 0 else (),
                                            inc="sPE" if fcc == FC - 1 else None)
                            mm_j[j] = pe_v
                        ws_guard[buf] = mm_j[3]
                        for j in range(4):
                            dc = dq * 4 + j
                            rsl = resid[:, dc * 2048 + t0 + tb * 512:
                                        dc * 2048 + t0 + tb * 512 + 512]
                            y_guard[j] = v_op(
                                lambda EE, _r=rsl, _p=pzf(j * 512, 512):
                                EE.tensor_tensor(_r, _p, _r, ALU.add),
                                [("sPE", mm_j[j])])
            p.join()

        pzb_discard = ab(C_PT[0], 1024)
        exp_disc = ab(12544, 1024)

        p.ost_dma = None
        p.cur_ld = None
        stage_n = [0]

        def stage():
            stage_n[0] += 1
            if STOP_AFTER is not None and stage_n[0] >= STOP_AFTER:
                raise _StopBuild()

        def emit_all():
            for a in range(2):
                if DBG and a == 1:
                    for c in range(DC):
                        p.dma("sync", dbg["dbg_r1"][c * 128:(c + 1) * 128, :],
                              resid[:, c * 2048:(c + 1) * 2048])
                    p.flush_sync()
                    p.join()
                pre = "a0" if a == 0 else "a1"
                bq, bk, bv, bo = (0, 1, 2, 3) if a == 0 else (None, None, None, None)
                prj["ws_pe"] = [None, None]
                prj["ev"] = [None, None]
                prj["stg_dma"] = [None, None]
                prj["vt_dma"] = None
                prj["vt_v"] = {}
                v_xln = emit_ln(0)
                emit_proj(a, pre + "kT", "k", 0, bk, v_xln)
                emit_proj(a, pre + "vT", "v", 0, bv, v_xln)
                emit_proj(a, pre + "qT", "q", 0, bq, v_xln)
                p.flush_sync()
                p.join()
                stage()
                v_xln = emit_ln(1)
                emit_proj(a, pre + "kT", "k", 1, bk, v_xln)
                emit_proj(a, pre + "vT", "v", 1, bv, v_xln)
                kv_waits = [("sKV0", N["sKV0"]), ("sKV1", N["sKV1"]), ("sVT", N["sVT"])]
                cc_v = p.op("gpsimd",
                            lambda EE, _a=a: EE.collective_compute(
                                "AllGather", ALU.bypass,
                                replica_groups=[[0, 1], [2, 3], [4, 5], [6, 7]],
                                ins=[kv_in[_a][:, :]],
                                outs=[kv_out[_a][:, :]]),
                            kv_waits, "sCC")
                emit_proj(a, pre + "qT", "q", 1, bq, v_xln)
                p.flush_sync()
                p.wait("sync", [("sCC", cc_v)])
                p.join()
                stage()
                if DBG and a == 0:
                    p.dma("sync", dbg["dbg_kvin0"][:, :], kv_in[0][:, :])
                    p.dma("sync", dbg["dbg_q0"][:, :], q_st[:, :])
                p.ost_dma = None
                emit_flash(a)
                if DBG:
                    p.flush_sync()
                    if a == 0:
                        p.dma("sync", dbg["dbg_ml"][:, 0:16], m_cols[:, :])
                        p.dma("sync", dbg["dbg_ml"][:, 16:32], l_cols[:, :])
                        p.dma("sync", dbg["dbg_ml"][:, 32:48], rcp_cols[:, :])
                    p.dma("sync", dbg["dbg_o0" if a == 0 else "dbg_o1"][:, :], o_st[:, :])
                    if a == 0:
                        p.dma("sync", dbg["dbg_kvout0"][:, :], kv_out[0][:, :])
                p.flush_sync()
                p.join()
                stage()
                emit_oproj(a, bo)
                if DBG:
                    for c in range(DC):
                        p.dma("sync", dbg["dbg_r0" if a == 0 else "dbg_r2"][c * 128:(c + 1) * 128, :],
                              resid[:, c * 2048:(c + 1) * 2048])
                    p.flush_sync()
                    p.join()
                stage()
                for th in range(2):
                    v_xln = emit_ln(th)
                    v_g = emit_router(th, v_xln)
                    stage()
                    emit_experts(th, v_xln, v_g)
                    stage()
                stage()

        try:
            emit_all()
        except _StopBuild:
            pass

        for c in range(DC):
            p.dma("sync", out_t[c * 128:(c + 1) * 128, :],
                  resid[:, c * 2048:(c + 1) * 2048])
        p.flush_sync()
        p.join()

        blk = en(nc.Block())

        @blk.sync
        def _(E_):
            for f in p.ops["sync"]:
                f(E_, sems)

        @blk.gpsimd
        def _(E_):
            for f in p.ops["gpsimd"]:
                f(E_, sems)

        @blk.tensor
        def _(E_):
            for f in p.ops["tensor"]:
                f(E_, sems)

        @blk.vector
        def _(E_):
            for f in p.ops["vector"]:
                f(E_, sems)

        @blk.scalar
        def _(E_):
            for f in p.ops["scalar"]:
                f(E_, sems)

    return nc


def _host_inputs(ins):
    bf = ml_dtypes.bfloat16
    base = {}
    base["a0qT"] = np.ascontiguousarray(ins["L0_attn_a_Wq"].T)
    base["a0kT"] = np.ascontiguousarray(ins["L0_attn_a_Wk"].T)
    base["a0vT"] = np.ascontiguousarray(ins["L0_attn_a_Wv"].T)
    base["a0oT"] = np.ascontiguousarray(ins["L0_attn_a_Wo"].T)
    base["a1qT"] = np.ascontiguousarray(ins["L0_attn_b_q_proj"].T)
    base["a1kT"] = np.ascontiguousarray(ins["L0_attn_b_k_proj"].T)
    base["a1vT"] = np.ascontiguousarray(ins["L0_attn_b_v_proj"].T)
    base["a1oT"] = np.ascontiguousarray(ins["L0_attn_b_o_proj"].T)
    for nm, key in (("bq", "L0_attn_a_bq"), ("bk", "L0_attn_a_bk"),
                    ("bv", "L0_attn_a_bv"), ("bo", "L0_attn_a_bo")):
        base["a0" + nm] = np.ascontiguousarray(
            np.asarray(ins[key], np.float32).reshape(16, 128).T)
    base["w1"] = np.ascontiguousarray(ins["L0_ffn_w1"])
    base["w2"] = np.ascontiguousarray(ins["L0_ffn_w2"])
    base["w3"] = np.ascontiguousarray(ins["L0_ffn_w3"])
    base["rwT"] = np.ascontiguousarray(np.asarray(ins["L0_ffn_router_w"]).T)
    base["cI_bf"] = np.eye(128, dtype=bf)
    base["cI_f32"] = np.eye(128, dtype=np.float32)
    base["cones_col"] = np.ones((128, 1), np.float32)
    base["cones_row"] = np.ones((1, 128), np.float32)
    io = np.broadcast_to(np.arange(8, dtype=np.float32), (128, 8))
    base["ciota"] = np.ascontiguousarray(io)
    base["ciotaB"] = np.ascontiguousarray(io + BIG)

    X = np.asarray(ins["L0_attn_a_X"], np.float32)
    qq = np.arange(128)[:, None]
    uu = np.arange(128)[None, :]
    mA_incl = np.where(uu > qq, NEG, 0.0).astype(bf)
    mA_excl = np.where(uu >= qq, NEG, 0.0).astype(bf)
    uB = np.arange(128)[:, None]
    qB = np.arange(128)[None, :]
    mB_incl = np.where(uB > qB, NEG, 0.0).astype(bf)
    mB_excl = np.where(uB >= qB, NEG, 0.0).astype(bf)

    in_maps = []
    for c in range(8):
        b, par = c // 2, c % 2
        m = dict(base)
        m["x"] = np.ascontiguousarray(X[b, par::2, :].T)
        mskA = np.empty((2, 128, 128), bf)
        mskB = np.empty((2, 128, 128), bf)
        for r in range(2):
            excl = (r != par and par == 0)
            mskA[r] = mA_excl if excl else mA_incl
            mskB[r] = mB_excl if excl else mB_incl
        m["cmaskA"] = mskA
        m["cmaskBd"] = mskB
        in_maps.append(m)
    return in_maps


def kernel(**inputs):
    in_maps = _host_inputs(inputs)
    nc = build_program()
    res = run_bass_kernel_spmd(nc, in_maps, core_ids=list(range(8)))
    out = np.empty((4, 4096, 2048), dtype=np.float32)
    for c in range(8):
        b, par = c // 2, c % 2
        out[b, par::2, :] = res.results[c]["out"].T
    return out


if __name__ == "__main__":
    nc = build_program()
    print("built:", nc.n_instructions(), "instructions")
